# version 01
# baseline (speedup 1.0000x reference)
import numpy as np
import ml_dtypes
from contextlib import ExitStack
from collections import defaultdict

import concourse.bass as bass
import concourse.mybir as mybir
from concourse.bass_utils import run_bass_kernel_spmd

F32 = mybir.dt.float32
BF16 = mybir.dt.bfloat16
AF = mybir.ActivationFunctionType
ALU = mybir.AluOpType
AX = mybir.AxisListType
BF = ml_dtypes.bfloat16

D = 1024
S = 2048
NT = S // 128
NCH = D // 128
NEG = -1e30



class Ctx:

    def __init__(self, nc):
        self.nc = nc
        self.plan = []
        self.sems = {}
        self.cnt = defaultdict(int)

    def make_sems(self, stack, names):
        for n in names:
            self.sems[n] = stack.enter_context(self.nc.semaphore(n))

    def op(self, eng, emit, inc=None, n=1):
        if inc is None:
            self.plan.append((eng, emit))
            return None
        self.cnt[inc] += n
        s = self.sems[inc]
        self.plan.append((eng, lambda E, emit=emit, s=s, n=n: emit(E).then_inc(s, n)))
        return self.cnt[inc]

    def wait(self, eng, sem, val=None):
        v = self.cnt[sem] if val is None else val
        if v <= 0:
            return
        s = self.sems[sem]
        self.plan.append((eng, lambda E, s=s, v=v: E.wait_ge(s, v)))

    def dma(self, eng, sem, out, in_, **kw):
        return self.op(eng, lambda E: E.dma_start(out=out, in_=in_, **kw), inc=sem, n=16)

    def store(self, out, in_):
        return self.op("sp", lambda E: E.dma_start(out=out, in_=in_), inc="st", n=16)

    def replay(self, block):
        plan = self.plan

        @block.sync
        def _(E):
            for g, fn in plan:
                if g == "sp":
                    fn(E)

        @block.gpsimd
        def _(E):
            for g, fn in plan:
                if g == "pool":
                    fn(E)

        @block.tensor
        def _(E):
            for g, fn in plan:
                if g == "pe":
                    fn(E)

        @block.vector
        def _(E):
            for g, fn in plan:
                if g == "dve":
                    fn(E)

        @block.scalar
        def _(E):
            for g, fn in plan:
                if g == "act":
                    fn(E)


def emit_ln_stats(ctx, bn6, sc1, src_ap, mcol, vcol, rscol, eps_ap):
    ctx.op("dve", lambda E: E.bn_stats(bn6[:, 0:6], src_ap[:, 0:512]))
    ctx.op("dve", lambda E: E.bn_stats(bn6[:, 6:12], src_ap[:, 512:1024]), inc="dve")
    ctx.wait("dve", "dve")
    c_mv = ctx.op("dve", lambda E: E.bn_aggr(mcol, bn6[:, 0:12]), inc="dve")
    ctx.wait("act", "dve", c_mv)
    ctx.op("act", lambda E: E.activation(sc1, vcol, AF.Ln, bias=eps_ap), inc="act")
    ctx.wait("act", "act")
    c_rs = ctx.op("act", lambda E: E.activation(rscol, sc1, AF.Exp, scale=-0.5),
                  inc="act")
    return c_mv, c_rs




def build_attn0():
    nc = bass.Bass("TRN2", target_bir_lowering=False)
    xin = nc.dram_tensor("x", [S, D], F32, kind="ExternalInput").ap()
    wq_d = nc.dram_tensor("wq", [8, D, 128], F32, kind="ExternalInput").ap()
    wk_d = nc.dram_tensor("wk", [8, D, 128], F32, kind="ExternalInput").ap()
    wv_d = nc.dram_tensor("wv", [8, D, 128], F32, kind="ExternalInput").ap()
    wo_d = nc.dram_tensor("wo", [D, D], F32, kind="ExternalInput").ap()
    gb_d = nc.dram_tensor("gb", [128, D], F32, kind="ExternalInput").ap()
    bb_d = nc.dram_tensor("bb", [128, D], F32, kind="ExternalInput").ap()
    mask_d = nc.dram_tensor("mask", [128, 4 * 512], BF16, kind="ExternalInput").ap()
    idn_d = nc.dram_tensor("idn", [128, 128], BF16, kind="ExternalInput").ap()
    yout = nc.dram_tensor("y", [S, D], F32, kind="ExternalOutput").ap()

    with ExitStack() as stk:
        e = stk.enter_context
        X = e(nc.sbuf_tensor("X", [128, NT * D], F32))
        XP = e(nc.sbuf_tensor("XP", [128, NCH * S], BF16))
        OF = e(nc.sbuf_tensor("OF", [128, NCH * S], BF16))
        WO = e(nc.sbuf_tensor("WO", [128, NCH * D], BF16))
        WQ = e(nc.sbuf_tensor("WQ", [128, 2 * NCH * 128], BF16))
        WK = e(nc.sbuf_tensor("WK", [128, 2 * NCH * 128], BF16))
        WV = e(nc.sbuf_tensor("WV", [128, 2 * NCH * 128], BF16))
        QP = e(nc.sbuf_tensor("QP", [128, S], BF16))
        KP = e(nc.sbuf_tensor("KP", [128, S], BF16))
        VP = e(nc.sbuf_tensor("VP", [128, S], BF16))
        PT = e(nc.sbuf_tensor("PT", [128, S], BF16))
        PTT = e(nc.sbuf_tensor("PTT", [128, S], BF16))
        XT = e(nc.sbuf_tensor("XT", [128, 2 * D], BF16))
        TMP = e(nc.sbuf_tensor("TMP", [128, D], F32))
        Y = e(nc.sbuf_tensor("Y", [128, D], F32))
        YN = e(nc.sbuf_tensor("YN", [128, D], F32))
        GB = e(nc.sbuf_tensor("GB", [128, D], F32))
        BBt = e(nc.sbuf_tensor("BBt", [128, D], F32))
        MASK = e(nc.sbuf_tensor("MASK", [128, 4 * 512], BF16))
        IDN = e(nc.sbuf_tensor("IDN", [128, 128], BF16))
        MV = e(nc.sbuf_tensor("MV", [128, 2 * NT], F32))
        RS = e(nc.sbuf_tensor("RS", [128, NT], F32))
        MV2 = e(nc.sbuf_tensor("MV2", [128, 2], F32))
        RS2 = e(nc.sbuf_tensor("RS2", [128, 1], F32))
        BN6 = e(nc.sbuf_tensor("BN6", [128, 12], F32))
        SC1 = e(nc.sbuf_tensor("SC1", [128, 1], F32))
        NGM = e(nc.sbuf_tensor("NGM", [128, 1], F32))
        SM = e(nc.sbuf_tensor("SM", [128, 1], F32))
        REC = e(nc.sbuf_tensor("REC", [128, 1], F32))
        EPS5 = e(nc.sbuf_tensor("EPS5", [128, 1], F32))
        EPS3 = e(nc.sbuf_tensor("EPS3", [128, 1], F32))

        proj_ps = e(nc.psum_tensor("proj_ps", [128, 2 * 512], F32))
        sc_ps = e(nc.psum_tensor("sc_ps", [128, 2048], F32))
        pt_ps = e(nc.psum_tensor("pt_ps", [128, 1024], BF16))
        pv_ps = e(nc.psum_tensor("pv_ps", [128, 128], F32))

        ctx = Ctx(nc)
        ctx.make_sems(stk, ["ld_x", "ld_c", "ld_wo", "lw0", "lw1", "st",
                            "pe", "dve", "act", "pool"])

        Xv = X[:, :].rearrange("p (i d) -> p i d", d=D)
        XPv = XP[:, :].rearrange("p (c t) -> p c t", t=S)
        OFv = OF[:, :].rearrange("p (c t) -> p c t", t=S)
        WOv = WO[:, :].rearrange("p (c d) -> p c d", d=D)

        c_x = ctx.dma("sp", "ld_x", Xv, xin.rearrange("(i p) d -> p i d", p=128))
        ctx.dma("sp", "ld_c", MASK[:, :], mask_d)
        ctx.dma("sp", "ld_c", IDN[:, :], idn_d)
        ctx.dma("sp", "ld_c", GB[:, :], gb_d)
        c_const = ctx.dma("sp", "ld_c", BBt[:, :], bb_d)
        c_wo = ctx.dma("pool", "ld_wo", WOv, wo_d.rearrange("(c p) d -> p c d", p=128))
        cw = {}
        for nmd, dst, src in (("q", WQ, wq_d), ("k", WK, wk_d), ("v", WV, wv_d)):
            cw[(0, nmd)] = ctx.dma(
                "pool", "lw0", dst[:, 0:1024].rearrange("p (c n) -> p c n", n=128),
                src[0].rearrange("(c p) n -> p c n", p=128))
        c_eps = ctx.op("dve", lambda E: E.memset(EPS5[:, :], 1e-5), inc="dve")
        c_eps = ctx.op("dve", lambda E: E.memset(EPS3[:, :], 1e-3), inc="dve")

        ctx.wait("dve", "ld_x", c_x)
        ctx.wait("dve", "ld_c", c_const)
        ctx.wait("pool", "ld_x", c_x)
        ctx.wait("pool", "ld_c", c_const)
        ctx.wait("act", "dve", c_eps)
        pe_tile_done = {}
        c_pt_evac = 0
        for i in range(NT):
            Xi = X[:, i * D:(i + 1) * D]
            c_mv, c_rs = emit_ln_stats(ctx, BN6, SC1[:, :], Xi,
                                       MV[:, 2 * i:2 * i + 2],
                                       MV[:, 2 * i + 1:2 * i + 2],
                                       RS[:, i:i + 1], EPS5[:, :])
            xt = XT[:, (i % 2) * D:(i % 2 + 1) * D]
            if i >= 2:
                ctx.wait("pool", "pe", pe_tile_done[i - 2])
            ctx.wait("pool", "dve", c_mv)
            ctx.wait("pool", "act", c_rs)
            c_ap = ctx.op("pool", lambda E, xt=xt, Xi=Xi, i=i:
                          E.tensor_scalar(xt, Xi, MV[:, 2 * i:2 * i + 1],
                                          RS[:, i:i + 1], ALU.subtract, ALU.mult),
                          inc="pool")
            ctx.wait("pe", "pool", c_ap)
            ctx.wait("pe", "ld_c", c_const)
            ctx.wait("pe", "dve", c_pt_evac)
            for c in range(NCH):
                r = ctx.op("pe", lambda E, c=c, xt=xt:
                           E.transpose(pt_ps[:, c * 128:(c + 1) * 128],
                                       xt[:, c * 128:(c + 1) * 128], IDN[:, :]),
                           inc="pe" if c == NCH - 1 else None)
            pe_tile_done[i] = r
            ctx.wait("dve", "pe", pe_tile_done[i])
            c_pt_evac = ctx.op(
                "dve", lambda E, i=i: E.tensor_copy(
                    XPv[:, :, 128 * i:128 * (i + 1)],
                    pt_ps[:, :].rearrange("p (c t) -> p c t", t=128)),
                inc="dve")

        pair_pe_done = {}
        for p in range(8):
            if p + 1 < 8:
                if p >= 1:
                    ctx.wait("pool", "pe", pair_pe_done[p - 1])
                b = (p + 1) % 2
                for nmd, dstT, srcT in (("q", WQ, wq_d), ("k", WK, wk_d),
                                        ("v", WV, wv_d)):
                    cw[(p + 1, nmd)] = ctx.dma(
                        "pool", f"lw{b}",
                        dstT[:, 1024 * b:1024 * (b + 1)]
                        .rearrange("p (c n) -> p c n", n=128),
                        srcT[p + 1].rearrange("(c p) n -> p c n", p=128))
            ctx.wait("pe", f"lw{p % 2}", cw[(p, "v")])
            ctx.wait("pe", "dve", c_pt_evac)
            bofs = (p % 2) * 1024
            for which, wsb, scale in (("q", WQ, 0.125), ("k", WK, None)):
                eng = "act" if which == "q" else "dve"
                for tb in range(4):
                    ps = proj_ps[:, (tb % 2) * 512:(tb % 2) * 512 + 512]
                    ctx.wait("pe", "act", ctx.cnt["act"])
                    ctx.wait("pe", "dve", ctx.cnt["dve"])
                    for c in range(NCH):
                        cg = ctx.op("pe", lambda E, c=c, tb=tb, wsb=wsb, ps=ps, bofs=bofs:
                                    E.matmul(ps, wsb[:, bofs + c * 128:bofs + (c + 1) * 128],
                                             XPv[:, c, 512 * tb:512 * (tb + 1)],
                                             start=(c == 0), stop=(c == NCH - 1)),
                                    inc="pe" if c == NCH - 1 else None)
                    ctx.wait(eng, "pe", cg)
                    if which == "q":
                        ctx.op("act", lambda E, tb=tb, ps=ps:
                               E.activation(QP[:, 512 * tb:512 * (tb + 1)], ps,
                                            AF.Copy, scale=0.125), inc="act")
                    else:
                        ctx.op("dve", lambda E, tb=tb, ps=ps:
                               E.tensor_copy(KP[:, 512 * tb:512 * (tb + 1)], ps),
                               inc="dve")
            for tt in range(NT):
                ps = proj_ps[:, (tt % 2) * 512:(tt % 2) * 512 + 128]
                ctx.wait("pe", "act", ctx.cnt["act"])
                ctx.wait("pe", "dve", ctx.cnt["dve"])
                for c in range(NCH):
                    cg = ctx.op("pe", lambda E, c=c, tt=tt, ps=ps, bofs=bofs:
                                E.matmul(ps, XPv[:, c, 128 * tt:128 * (tt + 1)],
                                         WV[:, bofs + c * 128:bofs + (c + 1) * 128],
                                         start=(c == 0), stop=(c == NCH - 1)),
                                inc="pe" if c == NCH - 1 else None)
                ctx.wait("dve", "pe", cg)
                ctx.op("dve", lambda E, tt=tt, ps=ps:
                       E.tensor_copy(VP[:, 128 * tt:128 * (tt + 1)], ps), inc="dve")
            c_qkv_dve = ctx.cnt["dve"]
            c_qkv_act = ctx.cnt["act"]

            for par in range(2):
                q_h = QP[64 * par:64 * (par + 1), :]
                k_h = KP[64 * par:64 * (par + 1), :]
                for i in range(NT):
                    jmax = i // 4
                    span = 512 * (jmax + 1)
                    ctx.wait("pe", "act", ctx.cnt["act"])
                    if par == 0 and i == 0:
                        ctx.wait("pe", "dve", c_qkv_dve)
                        ctx.wait("pe", "act", c_qkv_act)
                    for j in range(jmax + 1):
                        c_sc = ctx.op("pe", lambda E, i=i, j=j, q_h=q_h, k_h=k_h:
                                      E.matmul(sc_ps[:, 512 * j:512 * (j + 1)],
                                               q_h[:, 128 * i:128 * (i + 1)],
                                               k_h[:, 512 * j:512 * (j + 1)],
                                               start=True, stop=True),
                                      inc="pe" if j == jmax else None)
                    ctx.wait("dve", "pe", c_sc)
                    ctx.op("dve", lambda E, i=i, jmax=jmax:
                           E.tensor_tensor(sc_ps[:, 512 * jmax:512 * (jmax + 1)],
                                           sc_ps[:, 512 * jmax:512 * (jmax + 1)],
                                           MASK[:, 512 * (i % 4):512 * (i % 4 + 1)],
                                           ALU.add), inc="dve")
                    ctx.wait("dve", "dve")
                    c_ngm = ctx.op("dve", lambda E, span=span:
                                   E.reduce_max(NGM[:, :], sc_ps[:, 0:span],
                                                axis=AX.X, negate=True), inc="dve")
                    ctx.wait("act", "dve", c_ngm)
                    c_exp = ctx.op("act", lambda E, span=span:
                                   E.activation(PT[:, 0:span], sc_ps[:, 0:span],
                                                AF.Exp, bias=NGM[:, :],
                                                accum_out=SM[:, :]), inc="act")
                    ctx.wait("dve", "act", c_exp)
                    c_rec = ctx.op("dve", lambda E: E.reciprocal(REC[:, :], SM[:, :]),
                                   inc="dve")
                    ctx.wait("dve", "dve", c_rec)
                    c_nrm = ctx.op("dve", lambda E, span=span:
                                   E.tensor_scalar(PT[:, 0:span], PT[:, 0:span],
                                                   REC[:, :], None, ALU.mult),
                                   inc="dve")
                    nblk = i + 1
                    for w in range((nblk + 7) // 8):
                        lo, hi = 8 * w, min(8 * w + 8, nblk)
                        ctx.wait("pe", "dve", c_nrm if w == 0 else c_ptt)
                        for c in range(lo, hi):
                            c_tp = ctx.op("pe", lambda E, c=c, lo=lo:
                                          E.transpose(pt_ps[:, 128 * (c - lo):128 * (c - lo + 1)],
                                                      PT[:, 128 * c:128 * (c + 1)],
                                                      IDN[:, :]),
                                          inc="pe" if c == hi - 1 else None)
                        ctx.wait("dve", "pe", c_tp)
                        c_ptt = ctx.op("dve", lambda E, lo=lo, hi=hi: E.tensor_copy(
                            PTT[:, 128 * lo:128 * hi],
                            pt_ps[:, 0:128 * (hi - lo)]), inc="dve")
                    eng = "act" if (par + i) % 2 else "dve"
                    ctx.wait("pe", "dve", c_ptt)
                    ctx.wait("pe", eng, ctx.cnt[eng])
                    for c in range(nblk):
                        c_pv = ctx.op("pe", lambda E, c=c, par=par, i=i:
                                      E.matmul(pv_ps[0:64, 0:128],
                                               VP[:, 128 * c + 64 * par:128 * c + 64 * par + 64],
                                               PTT[:, 128 * c:128 * (c + 1)],
                                               start=(c == 0), stop=(c == i)),
                                      inc="pe" if c == i else None)
                    ctx.wait(eng, "pe", c_pv)
                    dst = OF[64 * par:64 * (par + 1),
                             S * p + 128 * i:S * p + 128 * (i + 1)]
                    if eng == "act":
                        ctx.op("act", lambda E, dst=dst:
                               E.copy(dst, pv_ps[0:64, 0:128]), inc="act")
                    else:
                        ctx.op("dve", lambda E, dst=dst:
                               E.tensor_copy(dst, pv_ps[0:64, 0:128]), inc="dve")
            pair_pe_done[p] = ctx.cnt["pe"]
        c_of_dve = ctx.cnt["dve"]
        c_of_act = ctx.cnt["act"]

        ctx.wait("pe", "ld_wo", c_wo)
        ctx.wait("pe", "dve", c_of_dve)
        ctx.wait("pe", "act", c_of_act)
        c_y = 0
        c_xu = 0
        for i in range(NT):
            Xi = X[:, i * D:(i + 1) * D]
            ctx.wait("pe", "dve", c_y)
            for hh in range(2):
                for c in range(NCH):
                    c_d = ctx.op("pe", lambda E, c=c, hh=hh, i=i:
                                 E.matmul(proj_ps[:, 512 * hh:512 * (hh + 1)],
                                          OFv[:, c, 128 * i:128 * (i + 1)],
                                          WOv[:, c, 512 * hh:512 * (hh + 1)],
                                          start=(c == 0), stop=(c == NCH - 1)),
                                 inc="pe" if (hh == 1 and c == NCH - 1) else None)
            ctx.wait("pool", "dve", c_y)
            c_tmp = ctx.op("pool", lambda E, Xi=Xi, i=i:
                           E.tensor_scalar(TMP[:, :], Xi, MV[:, 2 * i:2 * i + 1],
                                           RS[:, i:i + 1], ALU.subtract, ALU.mult),
                           inc="pool")
            ctx.wait("dve", "pe", c_d)
            ctx.wait("dve", "pool", c_tmp)
            c_y = ctx.op("dve", lambda E: E.tensor_add(Y[:, :], TMP[:, :],
                                                       proj_ps[:, 0:1024]), inc="dve")
            ctx.wait("dve", "dve")
            ctx.op("dve", lambda E: E.bn_stats(BN6[:, 0:6], Y[:, 0:512]))
            ctx.op("dve", lambda E: E.bn_stats(BN6[:, 6:12], Y[:, 512:1024]),
                   inc="dve")
            ctx.wait("dve", "dve")
            c_mv2 = ctx.op("dve", lambda E: E.bn_aggr(MV2[:, :], BN6[:, 0:12]),
                           inc="dve")
            ctx.wait("act", "dve", c_mv2)
            ctx.op("act", lambda E: E.activation(SC1[:, :], MV2[:, 1:2],
                                                 AF.Ln, bias=EPS3[:, :]), inc="act")
            ctx.wait("act", "act")
            c_rs2 = ctx.op("act", lambda E: E.activation(RS2[:, :], SC1[:, :],
                                                         AF.Exp, scale=-0.5),
                           inc="act")
            ctx.wait("dve", "act", c_rs2)
            ctx.wait("dve", "pool", c_xu)
            ctx.op("dve", lambda E: E.tensor_scalar(YN[:, :], Y[:, :], MV2[:, 0:1],
                                                    RS2[:, :], ALU.subtract,
                                                    ALU.mult), inc="dve")
            ctx.wait("pool", "dve", ctx.cnt["dve"])
            ctx.op("pool", lambda E: E.tensor_mul(YN[:, :], YN[:, :], GB[:, :]),
                   inc="pool")
            ctx.wait("pool", "pool")
            ctx.op("pool", lambda E: E.tensor_add(YN[:, :], YN[:, :], BBt[:, :]),
                   inc="pool")
            ctx.wait("pool", "pool")
            c_xu = ctx.op("pool", lambda E, Xi=Xi: E.tensor_add(Xi, Xi, YN[:, :]),
                          inc="pool")
        ctx.wait("sp", "pool", ctx.cnt["pool"])
        ctx.store(yout.rearrange("(i p) d -> p i d", p=128), Xv)
        ctx.wait("sp", "st", ctx.cnt["st"])

        with nc.Block() as block:
            ctx.replay(block)
    return nc




def _pairify(w):
    w = np.asarray(w, np.float32)
    return np.ascontiguousarray(
        np.transpose(w, (1, 0, 2)).reshape(D, D).reshape(D, 8, 128)
        .transpose(1, 0, 2))


def _prep_attn0(inp):
    p = np.arange(128)[:, None]
    f = np.arange(512)[None, :]
    mask = np.stack([np.where(f <= p + 128 * k, 0.0, NEG) for k in range(4)],
                    axis=1).reshape(128, 4 * 512).astype(BF)
    return {
        "wq": _pairify(inp["L0_attn_Wq"]),
        "wk": _pairify(inp["L0_attn_Wk"]),
        "wv": _pairify(inp["L0_attn_Wv"]),
        "wo": np.ascontiguousarray(np.asarray(inp["L0_attn_Wo"], np.float32)),
        "gb": np.ascontiguousarray(np.broadcast_to(
            np.asarray(inp["L0_attn_gamma"], np.float32), (128, D))),
        "bb": np.ascontiguousarray(np.broadcast_to(
            np.asarray(inp["L0_attn_beta"], np.float32), (128, D))),
        "mask": mask,
        "idn": np.eye(128, dtype=BF),
    }




def emit_load_ln_xp(ctx, nc, xin, B, eps_val, keep_stats):
    Xv, XPv = B["Xv"], B["XPv"]
    c_x = ctx.dma("sp", "ld_x", Xv, xin.rearrange("(i p) d -> p i d", p=128))
    ctx.op("dve", lambda E: E.memset(B["EPS"][:, :], eps_val), inc="dve")
    c_eps = ctx.cnt["dve"]
    ctx.wait("dve", "ld_x", c_x)
    ctx.wait("pool", "ld_x", c_x)
    ctx.wait("act", "dve", c_eps)
    pe_done = {}
    c_pt_evac = 0
    for i in range(NT):
        Xi = B["X"][:, i * D:(i + 1) * D]
        c_mv, c_rs = emit_ln_stats(ctx, B["BN6"], B["SC1"][:, :], Xi,
                                   B["MV"][:, 2 * i:2 * i + 2],
                                   B["MV"][:, 2 * i + 1:2 * i + 2],
                                   B["RS"][:, i:i + 1], B["EPS"][:, :])
        xt = B["XT"][:, (i % 2) * D:(i % 2 + 1) * D]
        if i >= 2:
            ctx.wait("pool", "pe", pe_done[i - 2])
        ctx.wait("pool", "dve", c_mv)
        ctx.wait("pool", "act", c_rs)
        c_ap = ctx.op("pool", lambda E, xt=xt, Xi=Xi, i=i:
                      E.tensor_scalar(xt, Xi, B["MV"][:, 2 * i:2 * i + 1],
                                      B["RS"][:, i:i + 1], ALU.subtract, ALU.mult),
                      inc="pool")
        ctx.wait("pe", "pool", c_ap)
        ctx.wait("pe", "ld_c", ctx.cnt["ld_c"])
        ctx.wait("pe", "dve", c_pt_evac)
        for c in range(NCH):
            r = ctx.op("pe", lambda E, c=c, xt=xt:
                       E.transpose(B["pt_ps"][:, c * 128:(c + 1) * 128],
                                   xt[:, c * 128:(c + 1) * 128], B["IDN"][:, :]),
                       inc="pe" if c == NCH - 1 else None)
        pe_done[i] = r
        ctx.wait("dve", "pe", pe_done[i])
        c_pt_evac = ctx.op(
            "dve", lambda E, i=i: E.tensor_copy(
                XPv[:, :, 128 * i:128 * (i + 1)],
                B["pt_ps"][:, :].rearrange("p (c t) -> p c t", t=128)),
            inc="dve")
    return c_x, c_pt_evac


def build_ffn0():
    nc = bass.Bass("TRN2", target_bir_lowering=False)
    xin = nc.dram_tensor("x", [S, D], F32, kind="ExternalInput").ap()
    rw_d = nc.dram_tensor("rw", [D, 16], F32, kind="ExternalInput").ap()
    gup_d = nc.dram_tensor("gup", [16, D, 2048], F32, kind="ExternalInput").ap()
    dwn_d = nc.dram_tensor("dwn", [16, D, D], F32, kind="ExternalInput").ap()
    shgu_d = nc.dram_tensor("shgu", [2, D, 2048], F32, kind="ExternalInput").ap()
    shdw_d = nc.dram_tensor("shdw", [2, D, D], F32, kind="ExternalInput").ap()
    idn_d = nc.dram_tensor("idn", [128, 128], BF16, kind="ExternalInput").ap()
    yout = nc.dram_tensor("y", [S, D], F32, kind="ExternalOutput").ap()

    with ExitStack() as stk:
        e = stk.enter_context
        X = e(nc.sbuf_tensor("X", [128, NT * D], F32))
        XP = e(nc.sbuf_tensor("XP", [128, NCH * S], BF16))
        GU = e(nc.sbuf_tensor("GU", [128, NCH * 2048], BF16))
        DW = e(nc.sbuf_tensor("DW", [128, 2 * NCH * D], BF16))
        Hb = e(nc.sbuf_tensor("Hb", [128, 8 * 512], BF16))
        SG = e(nc.sbuf_tensor("SG", [128, 512], BF16))
        XT = e(nc.sbuf_tensor("XT", [128, 2 * D], BF16))
        RW = e(nc.sbuf_tensor("RW", [128, NCH * 16], BF16))
        LG = e(nc.sbuf_tensor("LG", [128, NT * 16], F32))
        WF = e(nc.sbuf_tensor("WF", [128, NT * 16], F32))
        M1 = e(nc.sbuf_tensor("M1", [128, 1], F32))
        M2 = e(nc.sbuf_tensor("M2", [128, 1], F32))
        MSK1 = e(nc.sbuf_tensor("MSK1", [128, 16], F32))
        LG2 = e(nc.sbuf_tensor("LG2", [128, 16], F32))
        GTS = e(nc.sbuf_tensor("GTS", [128, 16], F32))
        IDN = e(nc.sbuf_tensor("IDN", [128, 128], BF16))
        MV = e(nc.sbuf_tensor("MV", [128, 2 * NT], F32))
        RS = e(nc.sbuf_tensor("RS", [128, NT], F32))
        BN6 = e(nc.sbuf_tensor("BN6", [128, 12], F32))
        SC1 = e(nc.sbuf_tensor("SC1", [128, 1], F32))
        EPS = e(nc.sbuf_tensor("EPS", [128, 1], F32))

        pt_ps = e(nc.psum_tensor("pt_ps", [128, 1024], BF16))
        g_ps = e(nc.psum_tensor("g_ps", [128, 512], F32))
        u_ps = e(nc.psum_tensor("u_ps", [128, 512], F32))
        d_ps = e(nc.psum_tensor("d_ps", [128, 2 * 1024], F32))
        m_ps = e(nc.psum_tensor("m_ps", [128, 512], F32))

        ctx = Ctx(nc)
        ctx.make_sems(stk, ["ld_x", "ld_c", "ld_rw", "lgu", "ldw0", "ldw1", "st",
                            "pe", "dve", "act", "pool"])
        B = {"X": X, "XP": XP, "XT": XT, "pt_ps": pt_ps, "BN6": BN6,
             "SC1": SC1, "MV": MV, "RS": RS, "EPS": EPS, "IDN": IDN,
             "Xv": X[:, :].rearrange("p (i d) -> p i d", d=D),
             "XPv": XP[:, :].rearrange("p (c t) -> p c t", t=S)}
        XPv = B["XPv"]
        GUv = GU[:, :].rearrange("p (c h) -> p c h", h=2048)

        ctx.dma("sp", "ld_c", IDN[:, :], idn_d)
        ctx.dma("pool", "ld_rw", RW[:, :].rearrange("p (c n) -> p c n", n=16),
                rw_d.rearrange("(c p) n -> p c n", p=128))

        def gu_src(ei):
            return shgu_d[ei] if ei < 2 else gup_d[ei - 2]

        def dw_src(ei):
            return shdw_d[ei] if ei < 2 else dwn_d[ei - 2]

        c_gu = {0: ctx.dma("pool", "lgu",
                           GUv, gu_src(0).rearrange("(c p) h -> p c h", p=128))}
        c_dw = {0: ctx.dma("pool", "ldw0",
                           DW[:, 0:NCH * D].rearrange("p (c h) -> p c h", h=D),
                           dw_src(0).rearrange("(c p) h -> p c h", h=D, p=128))}

        c_x, c_xp = emit_load_ln_xp(ctx, nc, xin, B, 1e-5, True)

        ctx.wait("pe", "ld_c", ctx.cnt["ld_c"])
        ctx.wait("pe", "ld_rw", ctx.cnt["ld_rw"])
        for i in range(NT):
            ctx.wait("pe", "dve", ctx.cnt["dve"])
            for c in range(NCH):
                r = ctx.op("pe", lambda E, c=c, i=i:
                           E.matmul(m_ps[:, 0:16], XPv[:, c, 128 * i:128 * (i + 1)],
                                    RW[:, 16 * c:16 * (c + 1)],
                                    start=(c == 0), stop=(c == NCH - 1)),
                           inc="pe" if c == NCH - 1 else None)
            ctx.wait("dve", "pe", r)
            ctx.op("dve", lambda E, i=i:
                   E.tensor_copy(LG[:, 16 * i:16 * (i + 1)], m_ps[:, 0:16]),
                   inc="dve")
        ctx.wait("act", "dve", ctx.cnt["dve"])
        c_wf = 0
        for i in range(NT):
            lg = LG[:, 16 * i:16 * (i + 1)]
            ctx.wait("act", "dve", c_wf)
            ctx.wait("dve", "dve")
            c_m1 = ctx.op("dve", lambda E, lg=lg:
                          E.reduce_max(M1[:, :], lg, axis=AX.X), inc="dve")
            ctx.wait("dve", "dve", c_m1)
            ctx.op("dve", lambda E, lg=lg:
                   E.tensor_scalar(MSK1[:, :], lg, M1[:, :], None, ALU.is_ge),
                   inc="dve")
            ctx.wait("dve", "dve")
            ctx.op("dve", lambda E, lg=lg:
                   E.scalar_tensor_tensor(LG2[:, :], MSK1[:, :], NEG, lg,
                                          ALU.mult, ALU.add), inc="dve")
            ctx.wait("dve", "dve")
            c_m2 = ctx.op("dve", lambda E:
                          E.reduce_max(M2[:, :], LG2[:, :], axis=AX.X), inc="dve")
            ctx.wait("dve", "dve", c_m2)
            c_msk = ctx.op("dve", lambda E, lg=lg:
                           E.tensor_scalar(MSK1[:, :], lg, M2[:, :], None,
                                           ALU.is_ge), inc="dve")
            c_sig = ctx.op("act", lambda E, lg=lg:
                           E.activation(GTS[:, :], lg, AF.Sigmoid), inc="act")
            ctx.wait("dve", "act", c_sig)
            ctx.wait("dve", "dve", c_msk)
            c_wf = ctx.op("dve", lambda E, i=i:
                          E.tensor_mul(WF[:, 16 * i:16 * (i + 1)], MSK1[:, :],
                                       GTS[:, :]), inc="dve")
        c_wf_all = c_wf

        c_dph = 0
        for ei in range(18):
            real = ei >= 2
            dwbuf = ei % 2
            ctx.wait("pe", "lgu", c_gu[ei])
            ctx.wait("pe", f"ldw{dwbuf}", c_dw[ei])
            ctx.wait("pe", "dve", c_xp)
            dwv = DW[:, dwbuf * NCH * D:(dwbuf + 1) * NCH * D] \
                .rearrange("p (c h) -> p c h", h=D)
            for tq in range(4):
                for f in range(NCH):
                    ctx.wait("pe", "act", ctx.cnt["act"])
                    ctx.wait("pe", "dve", ctx.cnt["dve"])
                    for c in range(NCH):
                        ctx.op("pe", lambda E, c=c, f=f, tq=tq:
                               E.matmul(g_ps[:, :], GUv[:, c, 128 * f:128 * (f + 1)],
                                        XPv[:, c, 512 * tq:512 * (tq + 1)],
                                        start=(c == 0), stop=(c == NCH - 1)))
                    for c in range(NCH):
                        r_u = ctx.op("pe", lambda E, c=c, f=f, tq=tq:
                                     E.matmul(u_ps[:, :],
                                              GUv[:, c, 1024 + 128 * f:1024 + 128 * (f + 1)],
                                              XPv[:, c, 512 * tq:512 * (tq + 1)],
                                              start=(c == 0), stop=(c == NCH - 1)),
                                     inc="pe" if c == NCH - 1 else None)
                    ctx.wait("act", "pe", r_u)
                    c_sg = ctx.op("act", lambda E:
                                  E.activation(SG[:, :], g_ps[:, :], AF.Silu),
                                  inc="act")
                    ctx.wait("dve", "pe", r_u)
                    ctx.wait("dve", "act", c_sg)
                    c_h = ctx.op("dve", lambda E, f=f:
                                 E.tensor_mul(Hb[:, 512 * f:512 * (f + 1)],
                                              u_ps[:, :], SG[:, :]), inc="dve")
                if tq == 3 and ei + 1 < 18:
                    ctx.wait("pool", "pe", ctx.cnt["pe"])
                    c_gu[ei + 1] = ctx.dma(
                        "pool", "lgu", GUv,
                        gu_src(ei + 1).rearrange("(c p) h -> p c h", p=128))
                ctx.wait("pe", "dve", c_h)
                if real:
                    ctx.wait("dve", "dve", c_wf_all)
                for ts in range(4):
                    it = 4 * tq + ts
                    dps = d_ps[:, (ts % 2) * 1024:(ts % 2) * 1024 + 1024]
                    ctx.wait("pe", "dve", c_dph)
                    for dh in range(2):
                        for f in range(NCH):
                            r_d = ctx.op(
                                "pe", lambda E, f=f, ts=ts, dh=dh, dps=dps, dwv=dwv:
                                E.matmul(dps[:, 512 * dh:512 * (dh + 1)],
                                         Hb[:, 512 * f + 128 * ts:512 * f + 128 * (ts + 1)],
                                         dwv[:, f, 512 * dh:512 * (dh + 1)],
                                         start=(f == 0), stop=(f == NCH - 1)),
                                inc="pe" if (dh == 1 and f == NCH - 1) else None)
                    ctx.wait("dve", "pe", r_d)
                    Xi = X[:, 1024 * it:1024 * (it + 1)]
                    if real:
                        wcol = WF[:, 16 * it + (ei - 2):16 * it + (ei - 2) + 1]
                        c_dph = ctx.op(
                            "dve", lambda E, Xi=Xi, dps=dps, wcol=wcol:
                            E.scalar_tensor_tensor(Xi, dps, wcol, Xi,
                                                   ALU.mult, ALU.add), inc="dve")
                    else:
                        c_dph = ctx.op("dve", lambda E, Xi=Xi, dps=dps:
                                       E.tensor_add(Xi, Xi, dps), inc="dve")
                ctx.wait("pe", "dve", c_dph)
            if ei + 1 < 18:
                ctx.wait("pool", "pe", ctx.cnt["pe"])
                c_dw[ei + 1] = ctx.dma(
                    "pool", f"ldw{(ei + 1) % 2}",
                    DW[:, ((ei + 1) % 2) * NCH * D:((ei + 1) % 2 + 1) * NCH * D]
                    .rearrange("p (c h) -> p c h", h=D),
                    dw_src(ei + 1).rearrange("(c p) h -> p c h", p=128))

        ctx.wait("sp", "dve", ctx.cnt["dve"])
        ctx.store(yout.rearrange("(i p) d -> p i d", p=128), B["Xv"])
        ctx.wait("sp", "st", ctx.cnt["st"])

        with nc.Block() as block:
            ctx.replay(block)
    return nc


def _prep_ffn0(inp):
    rw = np.ascontiguousarray(np.asarray(inp["L0_ffn_router_w"], np.float32).T)
    gup = np.ascontiguousarray(np.asarray(inp["L0_ffn_gate_up_w"], np.float32))
    dwn = np.ascontiguousarray(np.asarray(inp["L0_ffn_down_w"], np.float32))
    sgw = np.asarray(inp["L0_ffn_shared_gate_w"], np.float32).T
    suw = np.asarray(inp["L0_ffn_shared_up_w"], np.float32).T
    sdw = np.asarray(inp["L0_ffn_shared_down_w"], np.float32).T
    shgu = np.stack([
        np.concatenate([sgw[:, 0:1024], suw[:, 0:1024]], axis=1),
        np.concatenate([sgw[:, 1024:2048], suw[:, 1024:2048]], axis=1)])
    shdw = np.stack([sdw[0:1024], sdw[1024:2048]])
    return {"rw": rw, "gup": gup, "dwn": dwn,
            "shgu": np.ascontiguousarray(shgu),
            "shdw": np.ascontiguousarray(shdw),
            "idn": np.eye(128, dtype=BF)}


def run_ffn0(x_full, inp):
    nc = build_ffn0()
    shared = _prep_ffn0(inp)
    in_maps = [dict(shared, x=np.ascontiguousarray(x_full[b])) for b in range(8)]
    res = run_bass_kernel_spmd(nc, in_maps, core_ids=list(range(8)))
    return np.stack([res.results[b]["y"] for b in range(8)], axis=0)




def build_attn1():
    nc = bass.Bass("TRN2", target_bir_lowering=False)
    xin = nc.dram_tensor("x", [S, D], F32, kind="ExternalInput").ap()
    w3_d = nc.dram_tensor("w3", [128, 3 * 512], F32, kind="ExternalInput").ap()
    wo_d = nc.dram_tensor("wo", [D, D], F32, kind="ExternalInput").ap()
    idn_d = nc.dram_tensor("idn", [128, 128], BF16, kind="ExternalInput").ap()
    yout = nc.dram_tensor("y", [S, D], F32, kind="ExternalOutput").ap()

    with ExitStack() as stk:
        e = stk.enter_context
        X = e(nc.sbuf_tensor("X", [128, NT * D], F32))
        XQ = e(nc.sbuf_tensor("XQ", [128, NCH * S], BF16))
        KF = e(nc.sbuf_tensor("KF", [128, NT * D], BF16))
        VT = e(nc.sbuf_tensor("VT", [128, NT * D], BF16))
        WO = e(nc.sbuf_tensor("WO", [128, NCH * D], BF16))
        W3 = e(nc.sbuf_tensor("W3", [128, 3 * 512], BF16))
        KVZ = e(nc.sbuf_tensor("KVZ", [128, 16 * 65], BF16))
        ONES = e(nc.sbuf_tensor("ONES", [128, 1], BF16))
        MN = e(nc.sbuf_tensor("MN", [128, D], F32))
        EXs = e(nc.sbuf_tensor("EXs", [128, D], BF16))
        RL = e(nc.sbuf_tensor("RL", [128, D], BF16))
        QT = e(nc.sbuf_tensor("QT", [128, D], BF16))
        XT = e(nc.sbuf_tensor("XT", [128, 2 * D], BF16))
        DN = e(nc.sbuf_tensor("DN", [128, 1], F32))
        RC = e(nc.sbuf_tensor("RC", [128, 1], F32))
        EPS6 = e(nc.sbuf_tensor("EPS6", [128, 1], F32))
        IDN = e(nc.sbuf_tensor("IDN", [128, 128], BF16))
        MV = e(nc.sbuf_tensor("MV", [128, 2 * NT], F32))
        RS = e(nc.sbuf_tensor("RS", [128, NT], F32))
        BN6 = e(nc.sbuf_tensor("BN6", [128, 12], F32))
        SC1 = e(nc.sbuf_tensor("SC1", [128, 1], F32))
        EPS = e(nc.sbuf_tensor("EPS", [128, 1], F32))

        pt_ps = e(nc.psum_tensor("pt_ps", [128, 1024], BF16))
        p_ps = e(nc.psum_tensor("p_ps", [128, 1024], F32))
        kz_ps = e(nc.psum_tensor("kz_ps", [128, 130], F32))
        nd_ps = e(nc.psum_tensor("nd_ps", [128, 130], F32))
        d_ps = e(nc.psum_tensor("d_ps", [128, 1024], F32))

        ctx = Ctx(nc)
        ctx.make_sems(stk, ["ld_x", "ld_c", "ld_w", "ld_wo", "st",
                            "pe", "dve", "act", "pool"])
        B = {"X": X, "XP": XQ, "XT": XT, "pt_ps": pt_ps, "BN6": BN6,
             "SC1": SC1, "MV": MV, "RS": RS, "EPS": EPS, "IDN": IDN,
             "Xv": X[:, :].rearrange("p (i d) -> p i d", d=D),
             "XPv": XQ[:, :].rearrange("p (c t) -> p c t", t=S)}
        XPv = B["XPv"]
        WOv = WO[:, :].rearrange("p (c d) -> p c d", d=D)

        ctx.dma("sp", "ld_c", IDN[:, :], idn_d)
        c_w3 = ctx.dma("pool", "ld_w", W3[:, :], w3_d)
        c_wo = ctx.dma("pool", "ld_wo", WOv, wo_d.rearrange("(c p) d -> p c d", p=128))
        ctx.op("dve", lambda E: E.memset(ONES[:, :], 1.0), inc="dve")
        ctx.op("dve", lambda E: E.memset(EPS6[:, :], 1e-6), inc="dve")
        c_ones = ctx.cnt["dve"]

        c_x, c_xp = emit_load_ln_xp(ctx, nc, xin, B, 1e-5, False)

        ctx.wait("pe", "ld_w", c_w3)
        c_evac = 0
        c_pt = ctx.cnt["dve"]
        for i in range(NT):
            for m, nm in ((0, "q"), (1, "k"), (2, "v")):
                ctx.wait("pe", "dve", c_evac)
                ctx.wait("pe", "act", ctx.cnt["act"])
                for par in range(2):
                    for j in range(8):
                        h = 2 * j + par
                        pb = 64 * par
                        pc = 512 * par + 64 * j
                        r = ctx.op("pe", lambda E, h=h, pb=pb, pc=pc, m=m, i=i:
                                   E.matmul(p_ps[:, pc:pc + 64],
                                            XPv[pb:pb + 64, h // 2, 128 * i:128 * (i + 1)],
                                            W3[pb:pb + 64, 512 * m + 64 * (h // 2):
                                               512 * m + 64 * (h // 2) + 64],
                                            start=True, stop=True),
                                   inc="pe" if (par == 1 and j == 7) else None)
                if m < 2:
                    ctx.wait("dve", "pe", r)
                    c_mn = ctx.op("dve", lambda E:
                                  E.tensor_scalar_min(MN[:, :], p_ps[:, :], 0.0),
                                  inc="dve")
                    c_rl = ctx.op("dve", lambda E:
                                  E.tensor_scalar_max(RL[:, :], p_ps[:, :], 0.0),
                                  inc="dve")
                    ctx.wait("act", "dve", c_mn)
                    c_ex = ctx.op("act", lambda E:
                                  E.activation(EXs[:, :], MN[:, :], AF.Exp),
                                  inc="act")
                    ctx.wait("dve", "act", c_ex)
                    ctx.wait("dve", "dve", c_rl)
                    dst = QT[:, :] if m == 0 else KF[:, D * i:D * (i + 1)]
                    c_evac = ctx.op("dve", lambda E, dst=dst:
                                    E.tensor_add(dst, EXs[:, :], RL[:, :]),
                                    inc="dve")
                else:
                    ctx.wait("dve", "pe", r)
                    c_evac = ctx.op("dve", lambda E, i=i:
                                    E.tensor_copy(VT[:, D * i:D * (i + 1)],
                                                  p_ps[:, :]), inc="dve")
            for par in range(2):
                ctx.wait("pe", "dve", c_evac if par == 0 else c_pt)
                ctx.wait("pe", "dve", c_pt)
                for j in range(8):
                    h = 2 * j + par
                    pc = 512 * (h % 2) + 64 * (h // 2)
                    r = ctx.op("pe", lambda E, j=j, pc=pc:
                               E.transpose(pt_ps[0:64, 128 * j:128 * (j + 1)],
                                           QT[:, pc:pc + 64], IDN[:, :]),
                               inc="pe" if j == 7 else None)
                ctx.wait("dve", "pe", r)
                c_pt = ctx.op("dve", lambda E, par=par, i=i: E.tensor_copy(
                    XPv[64 * par:64 * par + 64, :, 128 * i:128 * (i + 1)],
                    pt_ps[0:64, :].rearrange("p (c t) -> p c t", t=128)),
                    inc="dve")
        c_proj = ctx.cnt["dve"]

        ctx.wait("pe", "dve", c_proj)
        ctx.wait("pe", "dve", c_ones)
        c_kvz_evac = 0
        for h in range(16):
            ko = (h % 2) * 65
            ctx.wait("pe", "dve", c_kvz_evac)
            pc = 512 * (h % 2) + 64 * (h // 2)
            for tt in range(NT):
                ctx.op("pe", lambda E, tt=tt, ko=ko, pc=pc:
                       E.matmul(kz_ps[0:64, ko:ko + 64],
                                KF[:, 1024 * tt + pc:1024 * tt + pc + 64],
                                VT[:, 1024 * tt + pc:1024 * tt + pc + 64],
                                start=(tt == 0), stop=(tt == NT - 1)))
            for tt in range(NT):
                r = ctx.op("pe", lambda E, tt=tt, ko=ko, pc=pc:
                           E.matmul(kz_ps[0:64, ko + 64:ko + 65],
                                    KF[:, 1024 * tt + pc:1024 * tt + pc + 64],
                                    ONES[:, :],
                                    start=(tt == 0), stop=(tt == NT - 1)),
                           inc="pe" if tt == NT - 1 else None)
            ctx.wait("dve", "pe", r)
            ctx.op("dve", lambda E, h=h, ko=ko:
                   E.tensor_copy(KVZ[0:64, 65 * h:65 * (h + 1)],
                                 kz_ps[0:64, ko:ko + 65]), inc="dve")
            c_kvz_evac = ctx.op("dve", lambda E, h=h, ko=ko:
                                E.tensor_copy(KVZ[64:128, 65 * h:65 * (h + 1)],
                                              kz_ps[0:64, ko:ko + 65]), inc="dve")
        c_kvz = c_kvz_evac

        c_ct = 0
        for i in range(NT):
            for h in range(16):
                pb = 64 * (h % 2)
                no = ((i * 16 + h) % 2) * 65
                ctx.wait("pe", "dve", c_ct)
                ctx.wait("pe", "dve", c_kvz)
                r = ctx.op("pe", lambda E, h=h, i=i, pb=pb, no=no:
                           E.matmul(nd_ps[:, no:no + 65],
                                    XPv[pb:pb + 64, h // 2, 128 * i:128 * (i + 1)],
                                    KVZ[pb:pb + 64, 65 * h:65 * (h + 1)],
                                    start=True, stop=True), inc="pe")
                ctx.wait("dve", "pe", r)
                ctx.op("dve", lambda E, no=no:
                       E.tensor_scalar(DN[:, :], nd_ps[:, no + 64:no + 65],
                                       EPS6[:, :], None, ALU.add), inc="dve")
                ctx.wait("dve", "dve")
                ctx.op("dve", lambda E: E.reciprocal(RC[:, :], DN[:, :]), inc="dve")
                ctx.wait("dve", "dve")
                c_ct = ctx.op("dve", lambda E, i=i, h=h, no=no:
                              E.tensor_scalar(KF[:, 1024 * i + 64 * h:
                                                 1024 * i + 64 * (h + 1)],
                                              nd_ps[:, no:no + 64],
                                              RC[:, :], None, ALU.mult), inc="dve")

        VTv = VT[:, :].rearrange("p (c t) -> p c t", t=S)
        c_pt2 = c_pt
        for i in range(NT):
            ctx.wait("pe", "dve", c_ct)
            ctx.wait("pe", "dve", c_pt2)
            for c in range(NCH):
                r = ctx.op("pe", lambda E, c=c, i=i:
                           E.transpose(pt_ps[:, c * 128:(c + 1) * 128],
                                       KF[:, 1024 * i + 128 * c:1024 * i + 128 * (c + 1)],
                                       IDN[:, :]),
                           inc="pe" if c == NCH - 1 else None)
            ctx.wait("dve", "pe", r)
            c_pt2 = ctx.op("dve", lambda E, i=i: E.tensor_copy(
                VTv[:, :, 128 * i:128 * (i + 1)],
                pt_ps[:, :].rearrange("p (c t) -> p c t", t=128)), inc="dve")
        c_ctf = c_pt2

        ctx.wait("pe", "ld_wo", c_wo)
        ctx.wait("pe", "dve", c_ctf)
        c_y = 0
        for i in range(NT):
            ctx.wait("pe", "dve", c_y)
            for hh in range(2):
                for c in range(NCH):
                    r = ctx.op("pe", lambda E, c=c, hh=hh, i=i:
                               E.matmul(d_ps[:, 512 * hh:512 * (hh + 1)],
                                        VTv[:, c, 128 * i:128 * (i + 1)],
                                        WOv[:, c, 512 * hh:512 * (hh + 1)],
                                        start=(c == 0), stop=(c == NCH - 1)),
                               inc="pe" if (hh == 1 and c == NCH - 1) else None)
            ctx.wait("dve", "pe", r)
            c_y = ctx.op("dve", lambda E, i=i:
                         E.tensor_add(X[:, 1024 * i:1024 * (i + 1)],
                                      X[:, 1024 * i:1024 * (i + 1)],
                                      d_ps[:, 0:1024]), inc="dve")

        ctx.wait("sp", "dve", ctx.cnt["dve"])
        ctx.store(yout.rearrange("(i p) d -> p i d", p=128), B["Xv"])
        ctx.wait("sp", "st", ctx.cnt["st"])

        with nc.Block() as block:
            ctx.replay(block)
    return nc


def _prep_attn1(inp):
    w3 = np.zeros((128, 3 * 512), np.float32)
    for m, nm in ((0, "L1_attn_Wq"), (1, "L1_attn_Wk"), (2, "L1_attn_Wv")):
        w = np.asarray(inp[nm], np.float32)
        for h in range(16):
            pb = 64 * (h % 2)
            w3[pb:pb + 64, 512 * m + 64 * (h // 2):512 * m + 64 * (h // 2) + 64] = w[h]
    return {"w3": w3,
            "wo": np.ascontiguousarray(np.asarray(inp["L1_attn_Wo"], np.float32)),
            "idn": np.eye(128, dtype=BF)}


def run_attn1(x_full, inp):
    nc = build_attn1()
    shared = _prep_attn1(inp)
    in_maps = [dict(shared, x=np.ascontiguousarray(x_full[b])) for b in range(8)]
    res = run_bass_kernel_spmd(nc, in_maps, core_ids=list(range(8)))
    return np.stack([res.results[b]["y"] for b in range(8)], axis=0)




def build_ffn1():
    nc = bass.Bass("TRN2", target_bir_lowering=False)
    xin = nc.dram_tensor("x", [S, D], F32, kind="ExternalInput").ap()
    rw_d = nc.dram_tensor("rw", [D, 8], F32, kind="ExternalInput").ap()
    br_d = nc.dram_tensor("br", [128, 8], F32, kind="ExternalInput").ap()
    w1_d = nc.dram_tensor("w1", [8, D, 4096], F32, kind="ExternalInput").ap()
    b1_d = nc.dram_tensor("b1", [128, 256], F32, kind="ExternalInput").ap()
    w2_d = nc.dram_tensor("w2", [8, 4096, D], F32, kind="ExternalInput").ap()
    b2_d = nc.dram_tensor("b2", [8, D], F32, kind="ExternalInput").ap()
    idn_d = nc.dram_tensor("idn", [128, 128], BF16, kind="ExternalInput").ap()
    yout = nc.dram_tensor("y", [S, D], F32, kind="ExternalOutput").ap()

    with ExitStack() as stk:
        e = stk.enter_context
        X = e(nc.sbuf_tensor("X", [128, NT * D], F32))
        XP = e(nc.sbuf_tensor("XP", [128, NCH * S], BF16))
        W1 = e(nc.sbuf_tensor("W1", [128, NCH * 2048], BF16))
        W2 = e(nc.sbuf_tensor("W2", [128, 16 * D], BF16))
        Hb = e(nc.sbuf_tensor("Hb", [128, 16 * 512], BF16))
        XT = e(nc.sbuf_tensor("XT", [128, 2 * D], BF16))
        RW = e(nc.sbuf_tensor("RW", [128, NCH * 8], BF16))
        BR = e(nc.sbuf_tensor("BR", [128, 8], F32))
        B1 = e(nc.sbuf_tensor("B1", [128, 256], F32))
        B2 = e(nc.sbuf_tensor("B2", [8, D], BF16))
        SCO = e(nc.sbuf_tensor("SCO", [128, NT * 8], F32))
        OH = e(nc.sbuf_tensor("OH", [128, NT * 8], F32))
        OHB = e(nc.sbuf_tensor("OHB", [128, 8], BF16))
        OHT = e(nc.sbuf_tensor("OHT", [8, S], BF16))
        M1 = e(nc.sbuf_tensor("M1", [128, 1], F32))
        IDN = e(nc.sbuf_tensor("IDN", [128, 128], BF16))
        MV = e(nc.sbuf_tensor("MV", [128, 2 * NT], F32))
        RS = e(nc.sbuf_tensor("RS", [128, NT], F32))
        BN6 = e(nc.sbuf_tensor("BN6", [128, 12], F32))
        SC1 = e(nc.sbuf_tensor("SC1", [128, 1], F32))
        EPS = e(nc.sbuf_tensor("EPS", [128, 1], F32))

        pt_ps = e(nc.psum_tensor("pt_ps", [128, 1024], BF16))
        h_ps = e(nc.psum_tensor("h_ps", [128, 2 * 512], F32))
        d_ps = e(nc.psum_tensor("d_ps", [128, 2 * 1024], F32))
        m_ps = e(nc.psum_tensor("m_ps", [128, 512], F32))

        ctx = Ctx(nc)
        ctx.make_sems(stk, ["ld_x", "ld_c", "ld_rw", "lw1", "lw2", "st",
                            "pe", "dve", "act", "pool"])
        B = {"X": X, "XP": XP, "XT": XT, "pt_ps": pt_ps, "BN6": BN6,
             "SC1": SC1, "MV": MV, "RS": RS, "EPS": EPS, "IDN": IDN,
             "Xv": X[:, :].rearrange("p (i d) -> p i d", d=D),
             "XPv": XP[:, :].rearrange("p (c t) -> p c t", t=S)}
        XPv = B["XPv"]
        W1v = W1[:, :].rearrange("p (c h) -> p c h", h=2048)
        W2v = W2[:, :].rearrange("p (c h) -> p c h", h=D)

        ctx.dma("sp", "ld_c", IDN[:, :], idn_d)
        ctx.dma("sp", "ld_c", BR[:, :], br_d)
        ctx.dma("sp", "ld_c", B1[:, :], b1_d)
        ctx.dma("pool", "ld_rw", RW[:, :].rearrange("p (c n) -> p c n", n=8),
                rw_d.rearrange("(c p) n -> p c n", p=128))
        ctx.dma("pool", "ld_rw", B2[:, :], b2_d)

        def w1_src(se):
            return w1_d[se // 2][:, 2048 * (se % 2):2048 * (se % 2 + 1)]

        def w2_src(se):
            return w2_d[se // 2][2048 * (se % 2):2048 * (se % 2 + 1), :]

        c_w1 = {0: ctx.dma("pool", "lw1", W1v,
                           w1_src(0).rearrange("(c p) h -> p c h", p=128))}
        c_w2 = {0: ctx.dma("pool", "lw2", W2v,
                           w2_src(0).rearrange("(c p) h -> p c h", p=128))}

        c_x, c_xp = emit_load_ln_xp(ctx, nc, xin, B, 1e-5, True)

        ctx.wait("pe", "ld_c", ctx.cnt["ld_c"])
        ctx.wait("pe", "ld_rw", ctx.cnt["ld_rw"])
        for i in range(NT):
            ctx.wait("pe", "dve", ctx.cnt["dve"])
            for c in range(NCH):
                r = ctx.op("pe", lambda E, c=c, i=i:
                           E.matmul(m_ps[:, 0:8], XPv[:, c, 128 * i:128 * (i + 1)],
                                    RW[:, 8 * c:8 * (c + 1)],
                                    start=(c == 0), stop=(c == NCH - 1)),
                           inc="pe" if c == NCH - 1 else None)
            ctx.wait("dve", "pe", r)
            ctx.op("dve", lambda E, i=i:
                   E.scalar_tensor_tensor(SCO[:, 8 * i:8 * (i + 1)], m_ps[:, 0:8],
                                          1.0, BR[:, :], ALU.mult, ALU.add),
                   inc="dve")
        c_pt = ctx.cnt["dve"]
        for i in range(NT):
            sco = SCO[:, 8 * i:8 * (i + 1)]
            ctx.wait("dve", "dve")
            c_m1 = ctx.op("dve", lambda E, sco=sco:
                          E.reduce_max(M1[:, :], sco, axis=AX.X), inc="dve")
            ctx.wait("dve", "dve", c_m1)
            ctx.op("dve", lambda E, sco=sco, i=i:
                   E.tensor_scalar(OH[:, 8 * i:8 * (i + 1)], sco, M1[:, :], None,
                                   ALU.is_ge), inc="dve")
            ctx.wait("dve", "dve")
            c_ohb = ctx.op("dve", lambda E, i=i:
                           E.tensor_copy(OHB[:, :], OH[:, 8 * i:8 * (i + 1)]),
                           inc="dve")
            ctx.wait("pe", "dve", c_ohb)
            ctx.wait("pe", "dve", c_pt)
            r = ctx.op("pe", lambda E:
                       E.transpose(pt_ps[0:8, 0:128], OHB[:, :], IDN[:, :]),
                       inc="pe")
            ctx.wait("dve", "pe", r)
            c_pt = ctx.op("dve", lambda E, i=i:
                          E.tensor_copy(OHT[0:8, 128 * i:128 * (i + 1)],
                                        pt_ps[0:8, 0:128]), inc="dve")
        c_oh_all = ctx.cnt["dve"]

        c_dph = 0
        for se in range(16):
            ex = se // 2
            ctx.wait("pe", "lw1", c_w1[se])
            ctx.wait("pe", "lw2", c_w2[se])
            ctx.wait("pe", "dve", c_xp)
            for tq in range(4):
                for f in range(16):
                    hp = h_ps[:, (f % 2) * 512:(f % 2) * 512 + 512]
                    ctx.wait("pe", "act", ctx.cnt["act"])
                    for c in range(NCH):
                        r_h = ctx.op("pe", lambda E, c=c, f=f, tq=tq, hp=hp:
                                     E.matmul(hp, W1v[:, c, 128 * f:128 * (f + 1)],
                                              XPv[:, c, 512 * tq:512 * (tq + 1)],
                                              start=(c == 0), stop=(c == NCH - 1)),
                                     inc="pe" if c == NCH - 1 else None)
                    ctx.wait("act", "pe", r_h)
                    bcol = 32 * ex + 16 * (se % 2) + f
                    ctx.op("act", lambda E, f=f, hp=hp, bcol=bcol:
                           E.activation(Hb[:, 512 * f:512 * (f + 1)], hp, AF.Gelu,
                                        bias=B1[:, bcol:bcol + 1]), inc="act")
                if tq == 3 and se + 1 < 16:
                    ctx.wait("pool", "pe", ctx.cnt["pe"])
                    c_w1[se + 1] = ctx.dma(
                        "pool", "lw1", W1v,
                        w1_src(se + 1).rearrange("(c p) h -> p c h", p=128))
                ctx.wait("pe", "act", ctx.cnt["act"])
                if se == 0 and tq == 0:
                    ctx.wait("dve", "dve", c_oh_all)
                for ts in range(4):
                    it = 4 * tq + ts
                    dps = d_ps[:, (ts % 2) * 1024:(ts % 2) * 1024 + 1024]
                    ctx.wait("pe", "dve", c_dph)
                    for dh in range(2):
                        for f in range(16):
                            r_d = ctx.op(
                                "pe", lambda E, f=f, ts=ts, dh=dh, dps=dps:
                                E.matmul(dps[:, 512 * dh:512 * (dh + 1)],
                                         Hb[:, 512 * f + 128 * ts:512 * f + 128 * (ts + 1)],
                                         W2v[:, f, 512 * dh:512 * (dh + 1)],
                                         start=(f == 0), stop=(f == 15)),
                                inc="pe" if (dh == 1 and f == 15) else None)
                    ctx.wait("dve", "pe", r_d)
                    Xi = X[:, 1024 * it:1024 * (it + 1)]
                    ocol = OH[:, 8 * it + ex:8 * it + ex + 1]
                    c_dph = ctx.op(
                        "dve", lambda E, Xi=Xi, dps=dps, ocol=ocol:
                        E.scalar_tensor_tensor(Xi, dps, ocol, Xi,
                                               ALU.mult, ALU.add), inc="dve")
                ctx.wait("pe", "dve", c_dph)
            if se + 1 < 16:
                ctx.wait("pool", "pe", ctx.cnt["pe"])
                c_w2[se + 1] = ctx.dma(
                    "pool", "lw2", W2v,
                    w2_src(se + 1).rearrange("(c p) h -> p c h", p=128))

        ctx.wait("pe", "ld_rw", ctx.cnt["ld_rw"])
        c_y = c_dph
        for i in range(NT):
            ctx.wait("pe", "dve", c_y)
            for dh in range(2):
                r = ctx.op("pe", lambda E, i=i, dh=dh:
                           E.matmul(d_ps[:, 512 * dh:512 * (dh + 1)],
                                    OHT[0:8, 128 * i:128 * (i + 1)],
                                    B2[0:8, 512 * dh:512 * (dh + 1)],
                                    start=True, stop=True),
                           inc="pe" if dh == 1 else None)
            ctx.wait("dve", "pe", r)
            c_y = ctx.op("dve", lambda E, i=i:
                         E.tensor_add(X[:, 1024 * i:1024 * (i + 1)],
                                      X[:, 1024 * i:1024 * (i + 1)],
                                      d_ps[:, 0:1024]), inc="dve")

        ctx.wait("sp", "dve", ctx.cnt["dve"])
        ctx.store(yout.rearrange("(i p) d -> p i d", p=128), B["Xv"])
        ctx.wait("sp", "st", ctx.cnt["st"])

        with nc.Block() as block:
            ctx.replay(block)
    return nc


def _prep_ffn1(inp):
    b1 = np.asarray(inp["L1_ffn_b1"], np.float32)
    b1sb = np.zeros((128, 256), np.float32)
    for ex in range(8):
        for col in range(32):
            b1sb[:, 32 * ex + col] = b1[ex, 128 * col:128 * (col + 1)]
    return {"rw": np.ascontiguousarray(np.asarray(inp["L1_ffn_wr"], np.float32)),
            "br": np.ascontiguousarray(np.broadcast_to(
                np.asarray(inp["L1_ffn_br"], np.float32), (128, 8))),
            "w1": np.ascontiguousarray(np.asarray(inp["L1_ffn_w1"], np.float32)),
            "b1": b1sb,
            "w2": np.ascontiguousarray(np.asarray(inp["L1_ffn_w2"], np.float32)),
            "b2": np.ascontiguousarray(np.asarray(inp["L1_ffn_b2"], np.float32)),
            "idn": np.eye(128, dtype=BF)}


def run_ffn1(x_full, inp):
    nc = build_ffn1()
    shared = _prep_ffn1(inp)
    in_maps = [dict(shared, x=np.ascontiguousarray(x_full[b])) for b in range(8)]
    res = run_bass_kernel_spmd(nc, in_maps, core_ids=list(range(8)))
    return np.stack([res.results[b]["y"] for b in range(8)], axis=0)


def run_attn0(x_full, inp):
    nc = build_attn0()
    shared = _prep_attn0(inp)
    in_maps = [dict(shared, x=np.ascontiguousarray(x_full[b])) for b in range(8)]
    res = run_bass_kernel_spmd(nc, in_maps, core_ids=list(range(8)))
    return np.stack([res.results[b]["y"] for b in range(8)], axis=0)


def kernel(**inp):
    x = np.asarray(inp["L0_attn_x"], dtype=np.float32)
    x = run_attn0(x, inp)
    x = run_ffn0(x, inp)
    x = run_attn1(x, inp)
    x = run_ffn1(x, inp)
    return np.ascontiguousarray(x)
